# version 35
# baseline (speedup 1.0000x reference)
import contextlib
import numpy as np
import ml_dtypes

import concourse.bass as bass
import concourse.mybir as mybir
from concourse.bass_utils import run_bass_kernel_spmd

F32 = mybir.dt.float32
BF16 = mybir.dt.bfloat16
AF = mybir.ActivationFunctionType
ALU = mybir.AluOpType
AX = mybir.AxisListType

N = 2048
D = 1024
NT = N // 128
NCH = D // 128
NM = N // 128
CW = 512
NCC = N // CW
EPS = 1e-5
NWS = 2
FLUSH_W0 = 62
FL_SEM = [0, 1, 2, 3, 4, 4, 5, 5, 5, 5, 5, 5, 5, 5]


class ESem:

    def __init__(self, handle):
        self.h = handle
        self.n = 0

    def inc(self, instr):
        instr.then_inc(self.h, 1)
        self.n += 1
        return self.n


class DSem:

    def __init__(self, handle):
        self.h = handle
        self.n = 0

    def inc(self, instr):
        instr.then_inc(self.h, 16)
        self.n += 16
        return self.n


def build_program(upto=99, dbg=False):
    nc = bass.Bass("TRN2", target_bir_lowering=False)

    x_d = nc.dram_tensor("x", [D, N], F32, kind="ExternalInput").ap()
    w_d = {}
    for wn in ("wq", "wk", "wv", "wp", "w1", "w2", "w3"):
        w_d[wn] = nc.dram_tensor(wn, [D, D], F32, kind="ExternalInput").ap()
    bias_d = nc.dram_tensor("bias", [D], F32, kind="ExternalInput").ap()
    identb_d = nc.dram_tensor("identb", [128, 128], BF16, kind="ExternalInput").ap()
    identf_d = nc.dram_tensor("identf", [128, 128], F32, kind="ExternalInput").ap()
    out_d = nc.dram_tensor("out", [128, NCH * N], F32, kind="ExternalOutput").ap()
    dbg_d = {}
    if dbg:
        for nm in ("dbg_q", "dbg_k", "dbg_s"):
            dbg_d[nm] = nc.dram_tensor(nm, [128, NCH * N], BF16, kind="ExternalOutput").ap()
        dbg_d["dbg_v"] = nc.dram_tensor("dbg_v", [128, NT * D], BF16, kind="ExternalOutput").ap()
        dbg_d["dbg_c"] = nc.dram_tensor("dbg_c", [128, 1], F32, kind="ExternalOutput").ap()

    ctx = contextlib.ExitStack()
    E = ctx.enter_context
    with ctx:
        xr = E(nc.sbuf_tensor("xr", [128, NCH * N], F32))
        S = E(nc.sbuf_tensor("S", [128, NCH * N], BF16))
        qb = E(nc.sbuf_tensor("qb", [128, NCH * N], BF16))
        kb = E(nc.sbuf_tensor("kb", [128, NCH * N], BF16))
        vt = E(nc.sbuf_tensor("vt", [128, NT * D], BF16))
        wbuf = E(nc.sbuf_tensor("wbuf", [128, NWS * 8 * 256], BF16))
        vtmp = E(nc.sbuf_tensor("vtmp", [128, 3072], BF16))
        identb = E(nc.sbuf_tensor("identb_sb", [128, 128], BF16))
        identf = E(nc.sbuf_tensor("identf_sb", [128, 8], F32))
        ones64 = E(nc.sbuf_tensor("ones64", [128, 64], BF16))
        ones128 = E(nc.sbuf_tensor("ones128", [128, 128], BF16))
        bias_pp = E(nc.sbuf_tensor("bias_pp_sb", [128, 8], F32))
        cbias = E(nc.sbuf_tensor("cbias", [128, 1], F32))
        tiny = E(nc.sbuf_tensor("tiny", [128, 8], F32))
        bias_st = E(nc.sbuf_tensor("bias_st", [8, 128], F32))

        S_ap = S.ap()
        qb_ap = qb.ap()
        kb_ap = kb.ap()
        xr_ap = xr.ap()
        Sf = S_ap.bitcast(F32)
        qbf = qb_ap.bitcast(F32)
        kbf = kb_ap.bitcast(F32)
        vtf = vt.ap().bitcast(F32)
        xr_j = xr_ap.rearrange("p (j n) -> p j n", j=NCH)
        vt_t = vt.ap().rearrange("p (t d) -> p t d", t=NT)

        s_te = ESem(E(nc.semaphore("s_te")))
        s_gp = ESem(E(nc.semaphore("s_gp")))
        gp_mark = [0]
        s_ve = ESem(E(nc.semaphore("s_ve")))
        s_se = ESem(E(nc.semaphore("s_se")))
        d_x = [DSem(E(nc.semaphore(f"d_x{i}"))) for i in range(4)]
        d_w = [DSem(E(nc.semaphore(f"d_w{i}"))) for i in range(NWS)]
        d_wf = [DSem(E(nc.semaphore(f"d_wf{i}"))) for i in range(6)]
        d_wh = [DSem(E(nc.semaphore(f"d_wh{i}"))) for i in range(4)]
        WH_SEM = [0, 1, 2, 3, 2, 3, 2, 3]
        d_o = [DSem(E(nc.semaphore(f"d_o{i}"))) for i in range(2)]
        d_misc = DSem(E(nc.semaphore("d_misc")))

        TE, VE, SE, SP, GP = nc.tensor, nc.vector, nc.scalar, nc.sync, nc.gpsimd
        _uid = [0]

        def _nm(base):
            _uid[0] += 1
            return f"{base}_{_uid[0]}"

        wq_order = []
        for wn in ("wq", "wk", "wv"):
            wq_order += [(wn, q) for q in range(4)]
        for _ in range(NCC):
            for wn in ("wp", "w1", "w2", "w3"):
                wq_order += [(wn, q) for q in range(4)]

        wslot_free = [0] * NWS

        def wslot(s):
            return wbuf[:, 2048 * s: 2048 * (s + 1)].rearrange("p (i d) -> p i d", i=8)

        def issue_wdma(idx):
            if idx >= FLUSH_W0 or idx >= len(wq_order):
                return
            wn, q = wq_order[idx]
            s = idx % NWS
            GP.wait_ge(s_te.h, wslot_free[s])
            src = w_d[wn].rearrange("(i p) d -> p i d", p=128)[:, :, 256 * q: 256 * (q + 1)]
            d_w[s].inc(GP.dma_start(wslot(s)[:], src))

        def vtslot(k):
            return vt.ap()[:, 2048 * k: 2048 * (k + 1)].rearrange("p (i d) -> p i d", i=8)

        def kbslot(k):
            return kb_ap[:, 2048 * k: 2048 * (k + 1)].rearrange("p (i d) -> p i d", i=8)

        def flslot(k):
            return kbslot(k) if k < 6 else vtslot(k - 6)

        def use_wquarter(idx):
            if idx < 8:
                s = WH_SEM[idx]
                nh = sum(1 for ss in WH_SEM if ss == s)
                TE.wait_ge(d_wh[s].h, 16 * nh)
                return vtslot(idx)
            if idx >= FLUSH_W0:
                k = idx - FLUSH_W0
                s = FL_SEM[k]
                nf = sum(1 for ss in FL_SEM if ss == s)
                TE.wait_ge(d_wf[s].h, 16 * nf)
                return flslot(k)
            s = idx % NWS
            nloads = sum(1 for i in range(8, idx + 1) if i % NWS == s)
            TE.wait_ge(d_w[s].h, 16 * nloads)
            return wslot(s)

        def done_wquarter(idx, te_mark):
            if idx < 8 or idx >= FLUSH_W0:
                return
            wslot_free[idx % NWS] = te_mark
            issue_wdma(idx + NWS)

        def preload_head_quarters():
            for k in range(8):
                wn, q = wq_order[k]
                src = w_d[wn].rearrange("(i p) d -> p i d",
                                        p=128)[:, :, 256 * q: 256 * (q + 1)]
                d_wh[WH_SEM[k]].inc(GP.dma_start(vtslot(k)[:], src))

        def _fl_load(k):
            wn, q = wq_order[FLUSH_W0 + k]
            src = w_d[wn].rearrange("(i p) d -> p i d",
                                    p=128)[:, :, 256 * q: 256 * (q + 1)]
            d_wf[FL_SEM[k]].inc(GP.dma_start(flslot(k)[:], src))

        def preload_flush_kb(gate_mark):
            GP.wait_ge(s_te.h, gate_mark)
            for k in range(6):
                _fl_load(k)

        def preload_flush_vt():
            GP.wait_ge(s_te.h, s_te.n)
            for k in range(6, len(wq_order) - FLUSH_W0):
                _fl_load(k)

        d_misc.inc(SP.dma_start(identb[:], identb_d))
        d_misc.inc(SP.dma_start(identf[:], identf_d[:, 0:8]))
        d_misc.inc(SP.dma_start(bias_st[:], bias_d.rearrange("(j p) -> j p", p=128)))
        VE.memset(ones64[:], 1.0)
        VE.memset(ones128[:], 1.0)
        VE.memset(tiny[:, 2:3], EPS)
        for e in (TE, VE, SE, SP):
            e.wait_ge(d_misc.h, d_misc.n)
        xsrc = x_d.rearrange("(jj q p) n -> jj p q n", p=128, q=2)
        for jj in range(4):
            d_x[jj].inc(SP.dma_start(xr_j[:, 2 * jj:2 * (jj + 1), :], xsrc[jj]))
        preload_head_quarters()
        issue_wdma(8)
        issue_wdma(9)
        with nc.psum_tensor("ps_bias", [128, 8], F32) as psb:
            mkb = s_te.inc(TE.transpose(psb[:], bias_st[:], identf[0:8, 0:8]))
            VE.wait_ge(s_te.h, mkb)
            mkb2 = s_ve.inc(VE.tensor_copy(bias_pp[:], psb[:]))
            TE.wait_ge(s_ve.h, mkb2)

        def layernorm(src_f32_j, dst_bf16, add_back, dma_gate=False, wq0=False):
            xbf = [qb_ap[:, 0:2048], qb_ap[:, 2048:4096]]
            xsq = [qb_ap[:, 4096:6144], qb_ap[:, 6144:8192]]
            mean_sb = qbf[:, 4096:6144]
            rstd_sb = qbf[:, 6144:8192]
            tmpf = qbf[:, 0:2048]
            tmpf2pp = [qbf[:, 2048:4096], kbf[:, 0:2048]]
            cast_mark = [0, 0]
            with nc.psum_tensor(_nm("ln_mean"), [128, 2048], F32) as mps, \
                 nc.psum_tensor(_nm("ln_sq"), [128, 2048], F32) as sps:
                mm_mark = [0, 0]
                for j in range(NCH):
                    sl = j % 2
                    if dma_gate:
                        VE.wait_ge(d_x[j // 2].h, 16)
                    if j >= 2:
                        VE.wait_ge(s_te.h, mm_mark[sl])
                    m1 = s_ve.inc(VE.tensor_copy(xbf[sl], src_f32_j[:, j, :]))
                    VE.wait_ge(s_ve.h, m1)
                    m2 = s_ve.inc(VE.tensor_mul(xsq[sl], xbf[sl], xbf[sl]))
                    TE.wait_ge(s_ve.h, m2)
                    for n4 in range(4):
                        TE.matmul(mps[:, 512 * n4:512 * (n4 + 1)], ones128[:],
                                  xbf[sl][:, 512 * n4:512 * (n4 + 1)],
                                  start=(j == 0), stop=(j == NCH - 1))
                        ins = TE.matmul(sps[:, 512 * n4:512 * (n4 + 1)], ones128[:],
                                        xsq[sl][:, 512 * n4:512 * (n4 + 1)],
                                        start=(j == 0), stop=(j == NCH - 1))
                    mm_mark[sl] = s_te.inc(ins)
                VE.wait_ge(s_te.h, s_te.n)
                m = s_ve.inc(VE.tensor_scalar(mean_sb, mps[:], 1.0 / D, None, ALU.mult))
                VE.wait_ge(s_ve.h, m)
                m = s_ve.inc(VE.tensor_mul(rstd_sb, mean_sb, mean_sb))
                VE.wait_ge(s_ve.h, m)
                m = s_ve.inc(VE.scalar_tensor_tensor(rstd_sb, sps[:], 1.0 / D, rstd_sb,
                                                     ALU.mult, ALU.subtract))
                SE.wait_ge(s_ve.h, m)
                m = s_se.inc(SE.activation(rstd_sb, rstd_sb, AF.Ln, bias=tiny[:, 2:3], scale=1.0))
                SE.wait_ge(s_se.h, m)
                m = s_se.inc(SE.activation(rstd_sb, rstd_sb, AF.Exp, bias=0.0, scale=-0.5))
                VE.wait_ge(s_se.h, m)
            def _normalize(wq0_pj):
                wsl0 = use_wquarter(0) if wq0_pj is not None else None
                mkq0 = 0
                deferred = []
                for j in range(NCH):
                    src_jj = src_f32_j[:, j, :]
                    dst_jj = dst_bf16[:, 2048 * j:2048 * (j + 1)]
                    m = s_ve.inc(VE.tensor_sub(tmpf, src_jj, mean_sb))
                    VE.wait_ge(s_ve.h, m)
                    if add_back:
                        sl2 = j % 2
                        if j >= 5:
                            buf = kbf[:, 2048 * (j - 4):2048 * (j - 3)]
                            m = s_ve.inc(VE.tensor_mul(buf, tmpf, rstd_sb))
                            SE.wait_ge(s_ve.h, m)
                            cast_mark[sl2] = s_se.inc(SE.copy(dst_jj, buf))
                            VE.wait_ge(s_ve.h, m)
                            deferred.append((src_jj, buf))
                        else:
                            if cast_mark[sl2]:
                                VE.wait_ge(s_se.h, cast_mark[sl2])
                            m = s_ve.inc(VE.tensor_mul(tmpf2pp[sl2], tmpf, rstd_sb))
                            SE.wait_ge(s_ve.h, m)
                            cast_mark[sl2] = s_se.inc(SE.copy(dst_jj, tmpf2pp[sl2]))
                            VE.wait_ge(s_ve.h, m)
                            m = s_ve.inc(VE.tensor_add(src_jj, src_jj, tmpf2pp[sl2]))
                            VE.wait_ge(s_ve.h, m)
                        if wq0_pj is not None:
                            TE.wait_ge(s_se.h, cast_mark[sl2])
                            for half in range(2):
                                for n4 in range(4):
                                    ins = TE.matmul(
                                        wq0_pj[half][:, 512 * n4:512 * (n4 + 1)],
                                        wsl0[:, j, 128 * half:128 * (half + 1)],
                                        dst_bf16[:, 2048 * j + 512 * n4:
                                                 2048 * j + 512 * (n4 + 1)],
                                        start=(j == 0), stop=(j == NCH - 1),
                                        skip_group_check=True)
                            mkq0 = s_te.inc(ins)
                    else:
                        m = s_ve.inc(VE.tensor_mul(dst_jj, tmpf, rstd_sb))
                        VE.wait_ge(s_ve.h, m)
                if wq0_pj is None:
                    nc.all_engine_barrier()
                    return None
                SE.wait_ge(s_ve.h, s_ve.n)
                SE.wait_ge(s_te.h, mkq0)
                ev0 = s_se.inc(SE.copy(qb_ap[:, 0:2048], wq0_pj[0][:]))
                ev1 = s_se.inc(SE.copy(qb_ap[:, 2048:4096], wq0_pj[1][:]))
                add_done = 0
                for src_jj, buf in deferred:
                    add_done = s_ve.inc(VE.tensor_add(src_jj, src_jj, buf))
                    VE.wait_ge(s_ve.h, add_done)
                return (ev0, ev1, add_done)

            if wq0:
                with nc.psum_tensor(_nm("pq0a"), [128, 2048], F32) as pq0a, \
                     nc.psum_tensor(_nm("pq0b"), [128, 2048], F32) as pq0b:
                    return _normalize([pq0a, pq0b])
            return _normalize(None)

        ln_evs = layernorm(xr_j, S_ap, add_back=True, dma_gate=True, wq0=True)
        if upto <= 1:
            _finish_dbg(nc, out_d, xr, dbg_d, [qb, kb, vt, S, cbias], tiny)
            return nc

        def wproj(widx0, src_bf, dst_bf, q0=0, ev_init=None):
            with nc.psum_tensor(_nm("pj0"), [128, 2048], F32) as p0, \
                 nc.psum_tensor(_nm("pj1"), [128, 2048], F32) as p1:
                pj = [p0, p1]
                ev_mark = [0, 0]
                first = True
                for q in range(q0, 4):
                    widx = widx0 + q
                    wsl = use_wquarter(widx)
                    for jj in range(2):
                        j = 2 * q + jj
                        sl = j % 2
                        if first and ev_init is not None:
                            TE.wait_ge(s_se.h, max(ev_init))
                        first = False
                        if ev_mark[sl]:
                            TE.wait_ge(s_se.h, ev_mark[sl])
                        for i in range(NCH):
                            for n4 in range(4):
                                ins = TE.matmul(
                                    pj[sl][:, 512 * n4:512 * (n4 + 1)],
                                    wsl[:, i, 128 * jj:128 * (jj + 1)],
                                    src_bf[:, 2048 * i + 512 * n4: 2048 * i + 512 * (n4 + 1)],
                                    start=(i == 0), stop=(i == NCH - 1))
                        mk = s_te.inc(ins)
                        SE.wait_ge(s_te.h, mk)
                        ev_mark[sl] = s_se.inc(SE.copy(dst_bf[:, 2048 * j:2048 * (j + 1)],
                                                       pj[sl][:]))
                        if jj == 1:
                            done_wquarter(widx, mk)
            nc.all_engine_barrier()

        wproj(0, S_ap, qb_ap, q0=1, ev_init=ln_evs[:2])

        rmax = tiny[:, 0:2]
        VE.memset(rmax, 0.0)
        qsq_pp = [vtmp[:, 0:1024], vtmp[:, 1024:2048]]
        with nc.psum_tensor("pk0", [128, 1024], F32) as pk0, \
             nc.psum_tensor("pk1", [128, 1024], F32) as pk1, \
             nc.psum_tensor("cq0", [128, 1024], F32) as cq0, \
             nc.psum_tensor("cq1", [128, 1024], F32) as cq1:
            pj = [pk0, pk1]
            cbq = [cq0, cq1]
            ev_mark = [0, 0]
            q4_sq = [0, 0]
            q4_mm = [0, 0]
            q4_red = [0, 0]
            q4_rm = [0]
            q4_next = [0]

            def p4q_sub():
                idx = q4_next[0]
                if idx >= 18:
                    return
                q4_next[0] += 1
                if idx < 16:
                    p, hf = divmod(idx, 2)
                    sl = idx % 2
                    if q4_mm[sl]:
                        SE.wait_ge(s_te.h, q4_mm[sl])
                    q4_sq[sl] = s_se.inc(SE.activation(
                        qsq_pp[sl], qb_ap[:, 2048 * p + 1024 * hf:
                                          2048 * p + 1024 * (hf + 1)],
                        AF.Square, bias=0.0, scale=1.0))
                if 0 <= idx - 1 < 16:
                    sl = (idx - 1) % 2
                    TE.wait_ge(s_se.h, q4_sq[sl])
                    if q4_red[sl]:
                        TE.wait_ge(s_ve.h, q4_red[sl])
                    for n2 in range(2):
                        ins = TE.matmul(cbq[sl][:, 512 * n2:512 * (n2 + 1)],
                                        ones128[:],
                                        qsq_pp[sl][:, 512 * n2:512 * (n2 + 1)],
                                        start=True, stop=True,
                                        skip_group_check=True)
                    q4_mm[sl] = s_te.inc(ins)
                if idx - 2 >= 0:
                    sl = (idx - 2) % 2
                    VE.wait_ge(s_te.h, q4_mm[sl])
                    if q4_rm[0]:
                        VE.wait_ge(s_ve.h, q4_rm[0])
                    m = s_ve.inc(VE.reduce_max(tiny[:, 6:7], cbq[sl][:], axis=AX.X))
                    VE.wait_ge(s_ve.h, m)
                    q4_rm[0] = s_ve.inc(VE.tensor_max(rmax[:, 0:1], rmax[:, 0:1],
                                                      tiny[:, 6:7]))
                    q4_red[sl] = q4_rm[0]

            for q in range(4):
                widx = 4 + q
                wsl = use_wquarter(widx)
                for jj in range(2):
                    j = 2 * q + jj
                    for hf in range(2):
                        sl = hf
                        if ev_mark[sl]:
                            TE.wait_ge(s_se.h, ev_mark[sl])
                        for i in range(NCH):
                            for n2 in range(2):
                                ins = TE.matmul(
                                    pj[sl][:, 512 * n2:512 * (n2 + 1)],
                                    wsl[:, i, 128 * jj:128 * (jj + 1)],
                                    S_ap[:, 2048 * i + 1024 * hf + 512 * n2:
                                         2048 * i + 1024 * hf + 512 * (n2 + 1)],
                                    start=(i == 0), stop=(i == NCH - 1),
                                    skip_group_check=True)
                        mk = s_te.inc(ins)
                        SE.wait_ge(s_te.h, mk)
                        if j == 0 and hf == 0 and ln_evs[2]:
                            SE.wait_ge(s_ve.h, ln_evs[2])
                        ev_mark[sl] = s_se.inc(SE.copy(
                            kb_ap[:, 2048 * j + 1024 * hf:
                                  2048 * j + 1024 * (hf + 1)], pj[sl][:]))
                        if jj == 1 and hf == 1:
                            done_wquarter(widx, mk)
                        p4q_sub()
                        p4q_sub()
            while q4_next[0] < 18:
                p4q_sub()
        nc.all_engine_barrier()
        if upto <= 2:
            _finish_dbg(nc, out_d, xr, dbg_d, [qb, kb, vt, S, cbias], tiny)
            return nc

        with nc.psum_tensor("pva", [128, 1024], F32) as pva, \
             nc.psum_tensor("pvb", [128, 1024], F32) as pvb, \
             nc.psum_tensor("tp0", [128, 512], BF16) as tp0, \
             nc.psum_tensor("tp1", [128, 512], BF16) as tp1, \
             nc.psum_tensor("tp2", [128, 512], BF16) as tp2, \
             nc.psum_tensor("tp3", [128, 512], BF16) as tp3:
            pvh = [pva, pvb]
            tps = [tp0, tp1, tp2, tp3]
            tp_mark = [0, 0, 0, 0]
            pv_mk = [0, 0]
            pv_evac_h = [0, 0]
            vtmp_half_done = [0, 0]
            for q in range(4):
                widx = 8 + q
                wsl = use_wquarter(widx)
                for jj in range(2):
                    j = 2 * q + jj
                    for h in range(2):
                        if pv_evac_h[h]:
                            TE.wait_ge(s_ve.h, pv_evac_h[h])
                        for i in range(NCH):
                            for n2 in range(2):
                                ins = TE.matmul(
                                    pvh[h][:, 512 * n2:512 * (n2 + 1)],
                                    wsl[:, i, 128 * jj:128 * (jj + 1)],
                                    S_ap[:, 2048 * i + 1024 * h + 512 * n2:
                                         2048 * i + 1024 * h + 512 * (n2 + 1)],
                                    start=(i == 0), stop=(i == NCH - 1),
                                    skip_group_check=True)
                        pv_mk[h] = s_te.inc(ins)
                        if jj == 1 and h == 1:
                            done_wquarter(widx, pv_mk[h])
                        VE.wait_ge(s_te.h, pv_mk[h])
                        if vtmp_half_done[h]:
                            VE.wait_ge(s_te.h, vtmp_half_done[h])
                        pv_evac_h[h] = s_ve.inc(
                            VE.tensor_copy(vtmp[:, 1024 * h:1024 * (h + 1)], pvh[h][:]))
                    for h in range(2):
                        TE.wait_ge(s_ve.h, pv_evac_h[h])
                        for sg2 in range(2):
                            sgrp = 2 * h + sg2
                            sl = sgrp % 4
                            if tp_mark[sl]:
                                TE.wait_ge(s_ve.h, tp_mark[sl])
                            for b in range(4):
                                tt = 4 * sgrp + b
                                ins = TE.transpose(tps[sl][:, 128 * b:128 * (b + 1)],
                                                   vtmp[:, 128 * tt:128 * (tt + 1)], identb[:])
                            mk2 = s_te.inc(ins)
                            if sg2 == 1:
                                vtmp_half_done[h] = mk2
                            VE.wait_ge(s_te.h, mk2)
                            tp_mark[sl] = s_ve.inc(
                                VE.tensor_copy(vt_t[:, 4 * sgrp:4 * (sgrp + 1),
                                                    128 * j:128 * (j + 1)],
                                               tps[sl][:]))
        if upto <= 3:
            nc.all_engine_barrier()
            _finish_dbg(nc, out_d, xr, dbg_d, [qb, kb, vt, S, cbias], tiny)
            return nc

        qsqs = [S_ap[:, 0:2048], S_ap[:, 2048:4096]]
        SE.wait_ge(s_te.h, s_te.n)
        TE.wait_ge(s_ve.h, s_ve.n)
        with nc.psum_tensor("cb0", [128, 2048], F32) as cb0, \
             nc.psum_tensor("cb1", [128, 2048], F32) as cb1:
            cbs = [cb0, cb1]
            p4_mm = [0, 0]
            p4_sq = [0, 0]
            p4_red = [0, 0]
            p4_rm = [0]

            def p4_square(step):
                sl = step % 2
                if p4_mm[sl]:
                    SE.wait_ge(s_te.h, p4_mm[sl])
                p4_sq[sl] = s_se.inc(SE.activation(
                    qsqs[sl], kb_ap[:, 2048 * step:2048 * (step + 1)], AF.Square,
                    bias=0.0, scale=1.0))

            def p4_mme(step):
                sl = step % 2
                TE.wait_ge(s_se.h, p4_sq[sl])
                if p4_red[sl]:
                    TE.wait_ge(s_ve.h, p4_red[sl])
                for n4 in range(4):
                    ins = TE.matmul(cbs[sl][:, 512 * n4:512 * (n4 + 1)], ones128[:],
                                    qsqs[sl][:, 512 * n4:512 * (n4 + 1)],
                                    start=True, stop=True)
                p4_mm[sl] = s_te.inc(ins)

            def p4_reduce(step):
                sl = step % 2
                VE.wait_ge(s_te.h, p4_mm[sl])
                if p4_rm[0]:
                    VE.wait_ge(s_ve.h, p4_rm[0])
                m = s_ve.inc(VE.reduce_max(tiny[:, 6:7], cbs[sl][:], axis=AX.X))
                VE.wait_ge(s_ve.h, m)
                p4_rm[0] = s_ve.inc(VE.tensor_max(rmax[:, 1:2],
                                                  rmax[:, 1:2], tiny[:, 6:7]))
                p4_red[sl] = p4_rm[0]

            for step in range(10):
                if step < NCH:
                    p4_square(step)
                if 1 <= step <= NCH:
                    p4_mme(step - 1)
                if step >= 2:
                    p4_reduce(step - 2)
            VE.wait_ge(s_ve.h, p4_rm[0])
            m = s_ve.inc(VE.tensor_add(tiny[:, 1:2], rmax[:, 0:1], rmax[:, 1:2]))
            VE.wait_ge(s_ve.h, m)
            cbias_mark = s_ve.inc(VE.tensor_scalar(cbias[:], tiny[:, 1:2],
                                                   -1.0 / 16.0, None, ALU.mult))
        if upto <= 4:
            nc.all_engine_barrier()
            _finish_dbg(nc, out_d, xr, dbg_d, [qb, kb, vt, S, cbias], tiny)
            return nc

        p_un = [S_ap[:, 0:1024], S_ap[:, 1024:2048]]
        ao = S_ap[:, 2048:6144]
        dr = Sf[:, 3072:3584]
        ao_un = Sf[:, 3584:4096]
        h2st = S_ap[:, 8192:12288]
        xl2 = S_ap[:, 12288:16384]
        vtmpf = vtmp.ap().bitcast(F32)
        xbf = vtmp[:, 0:512]
        xsq = vtmp[:, 512:1024]
        mean_c = vtmpf[:, 512:1024]
        rstd_c = vtmpf[:, 1024:1536]
        ostage = [vtmpf[:, 0:512], vtmpf[:, 512:1024]]

        iters = [(c, p, m) for c in range(NCC) for p in range(NCH) for m in range(NM)]
        with nc.psum_tensor("sc0", [128, 1024], F32) as sc0, \
             nc.psum_tensor("sc1", [128, 1024], F32) as sc1, \
             nc.psum_tensor("pvs", [128, 512], F32) as pvs, \
             nc.psum_tensor("dns", [128, 512], F32) as dns, \
             nc.psum_tensor("wps", [128, 512], F32) as wps, \
             nc.psum_tensor("lnps", [128, 512], F32) as lnps:
            scs = [sc0, sc1]
            exp_slot = [0, 0]
            den_slot = [0, 0]
            qk_mark = {}
            pv_evac_mark = 0
            recip_mark = 0
            ao_mult_mark = 0
            st = {
                "wp_mm": 0,
                "wp_ev": 0,
                "wps_free": (s_ve, 0),
                "xbf_free": 0,
                "sq_done": [0, 0],
                "sq_mm": [0, 0],
                "mean_ev": 0,
                "var_ev": 0,
                "rstd": 0,
                "apply": 0,
                "apply_g": [0] * NCH,
                "h1_ev": 0,
                "h2_ev": 0,
                "h3_cp": 0,
                "h3_mm": 0,
                "tanh": 0,
                "add": 0,
                "lnps_free": (s_ve, 0),
                "out_cp": 0,
            }

            st_sems = [d_o[0], d_o[1], d_x[0], d_x[1]]

            def wait_mark(eng, pair):
                sem, mark = pair
                if mark:
                    eng.wait_ge(sem.h, mark)

            def emit_qk(t):
                c, p, m = iters[t]
                sl = m % 2
                if exp_slot[sl]:
                    TE.wait_ge(s_se.h, exp_slot[sl])
                TE.matmul(scs[sl][:, 0:512],
                          kb[0:64, 2048 * p + 128 * m: 2048 * p + 128 * (m + 1)],
                          qb[0:64, 2048 * p + 512 * c: 2048 * p + 512 * (c + 1)],
                          start=True, stop=True, tile_position=(0, 0))
                ins = TE.matmul(scs[sl][:, 512:1024],
                                kb[64:128, 2048 * p + 128 * m: 2048 * p + 128 * (m + 1)],
                                qb[64:128, 2048 * p + 512 * c: 2048 * p + 512 * (c + 1)],
                                start=True, stop=True, tile_position=(64, 0))
                qk_mark[t] = s_te.inc(ins)

            accs = [wps, lnps]
            acc_free = [(s_ve, 0), (s_ve, 0)]
            nacc = [2]

            def emit_wp_j(cw, j):
                widx = 12 + 16 * cw + j // 2
                wsl = use_wquarter(widx)
                jj = j % 2
                acc = accs[j % nacc[0]]
                wait_mark(TE, acc_free[j % nacc[0]])
                TE.wait_ge(s_ve.h, ao_ready[cw])
                for i in range(NCH):
                    ins = TE.matmul(acc[:], wsl[:, i, 128 * jj:128 * (jj + 1)],
                                    ao[:, 512 * i:512 * (i + 1)],
                                    start=(i == 0), stop=(i == NCH - 1))
                st["wp_mm"] = s_te.inc(ins)
                if jj == 1:
                    done_wquarter(widx, st["wp_mm"])
                VE.wait_ge(s_te.h, st["wp_mm"])
                xs = xr_j[:, j, 512 * cw:512 * (cw + 1)]
                st["wp_ev"] = s_ve.inc(VE.tensor_add(xs, xs, acc[:]))
                acc_free[j % nacc[0]] = (s_ve, st["wp_ev"])

            def emit_stats_j(cw, which, j):
                acc = accs[1]
                if j == 0 and which == 0:
                    while wp_q:
                        wp_q.pop(0)()
                if st["xbf_free"]:
                    VE.wait_ge(s_te.h, st["xbf_free"])
                if j == 0 and which == 0:
                    VE.wait_ge(s_ve.h, st["wp_ev"])
                mcast = s_ve.inc(VE.tensor_copy(xbf, xr_j[:, j, 512 * cw:512 * (cw + 1)]))
                src = xbf
                if which == 1:
                    VE.wait_ge(s_ve.h, mcast)
                    mcast = s_ve.inc(VE.tensor_mul(xsq, xbf, xbf))
                    src = xsq
                TE.wait_ge(s_ve.h, mcast)
                if j == 0:
                    wait_mark(TE, acc_free[1] if which == 0 else (s_ve, st["mean_ev"]))
                ins = TE.matmul(acc[:], ones128[:], src,
                                start=(j == 0), stop=(j == NCH - 1))
                st["xbf_free"] = s_te.inc(ins)

            def emit_stats1_j(cw, j):
                sl = j % 2
                xb = [vtmp[:, 0:512], vtmp[:, 512:1024]][sl]
                xq = [vtmp[:, 1024:1536], vtmp[:, 1536:2048]][sl]
                if j == 0:
                    while wp_q:
                        wp_q.pop(0)()
                    VE.wait_ge(s_ve.h, st["wp_ev"])
                if st["sq_mm"][sl]:
                    VE.wait_ge(s_te.h, st["sq_mm"][sl])
                if st["sq_done"][sl]:
                    VE.wait_ge(s_se.h, st["sq_done"][sl])
                mcast = s_ve.inc(VE.tensor_copy(xb, xr_j[:, j, 512 * cw:512 * (cw + 1)]))
                TE.wait_ge(s_ve.h, mcast)
                if j == 0:
                    wait_mark(TE, acc_free[1])
                    wait_mark(TE, acc_free[3])
                ins = TE.matmul(accs[1][:], ones128[:], xb,
                                start=(j == 0), stop=(j == NCH - 1))
                s_te.inc(ins)
                SE.wait_ge(s_ve.h, mcast)
                st["sq_done"][sl] = s_se.inc(SE.activation(xq, xb, AF.Square,
                                                           bias=0.0, scale=1.0))
                TE.wait_ge(s_se.h, st["sq_done"][sl])
                ins = TE.matmul(accs[3][:], ones128[:], xq,
                                start=(j == 0), stop=(j == NCH - 1))
                st["sq_mm"][sl] = s_te.inc(ins)
                st["xbf_free"] = st["sq_mm"][sl]

            def emit_stats1_fin(cw):
                VE.wait_ge(s_te.h, st["xbf_free"])
                st["mean_ev"] = s_ve.inc(VE.tensor_scalar(mean_c, lnps[:], 1.0 / D,
                                                          None, ALU.mult))
                VE.wait_ge(s_ve.h, st["mean_ev"])
                m1 = s_ve.inc(VE.tensor_mul(rstd_c, mean_c, mean_c))
                VE.wait_ge(s_ve.h, m1)
                st["var_ev"] = s_ve.inc(VE.scalar_tensor_tensor(
                    rstd_c, accs[3][:], 1.0 / D, rstd_c, ALU.mult, ALU.subtract))
                acc_free[1] = (s_ve, st["var_ev"])
                acc_free[3] = (s_ve, st["var_ev"])

            def emit_stats_fin(cw, which):
                VE.wait_ge(s_te.h, st["xbf_free"])
                if which == 0:
                    st["mean_ev"] = s_ve.inc(VE.tensor_scalar(mean_c, lnps[:], 1.0 / D,
                                                              None, ALU.mult))
                else:
                    VE.wait_ge(s_ve.h, st["mean_ev"])
                    m1 = s_ve.inc(VE.tensor_mul(rstd_c, mean_c, mean_c))
                    VE.wait_ge(s_ve.h, m1)
                    st["var_ev"] = s_ve.inc(VE.scalar_tensor_tensor(
                        rstd_c, lnps[:], 1.0 / D, rstd_c, ALU.mult, ALU.subtract))
                    acc_free[1] = (s_ve, st["var_ev"])

            def emit_rstd(cw):
                SE.wait_ge(s_ve.h, st["var_ev"])
                m1 = s_se.inc(SE.activation(rstd_c, rstd_c, AF.Ln,
                                            bias=tiny[:, 2:3], scale=1.0))
                SE.wait_ge(s_se.h, m1)
                st["rstd"] = s_se.inc(SE.activation(rstd_c, rstd_c, AF.Exp,
                                                    bias=0.0, scale=-0.5))

            def emit_apply(cw, jlist):
                tmpf32 = vtmpf[:, 0:512]
                for j in jlist:
                    VE.wait_ge(s_se.h, st["rstd"])
                    if st["xbf_free"]:
                        VE.wait_ge(s_te.h, st["xbf_free"])
                        st["xbf_free"] = 0
                    if st["apply"]:
                        VE.wait_ge(s_ve.h, st["apply"])
                    m1 = s_ve.inc(VE.tensor_sub(tmpf32, xr_j[:, j, 512 * cw:512 * (cw + 1)],
                                                mean_c))
                    VE.wait_ge(s_ve.h, m1)
                    st["apply"] = s_ve.inc(VE.tensor_mul(xl2[:, 512 * j:512 * (j + 1)],
                                                         tmpf32, rstd_c))
                    st["apply_g"][j] = st["apply"]

            def emit_proj_j(cw, j, wbase, rhs_of, dst_of, ev_key, gate_i=False):
                widx = 12 + 16 * cw + wbase + j // 2
                wsl = use_wquarter(widx)
                jj = j % 2
                acc = accs[j % nacc[0]]
                wait_mark(TE, acc_free[j % nacc[0]])
                if ev_key == "h1_ev" and j == 0 and not gate_i:
                    TE.wait_ge(s_ve.h, st["apply"])
                if ev_key == "h2_ev" and j == 0:
                    TE.wait_ge(s_ve.h, st["h1_ev"])
                    if st["h3_mm"]:
                        TE.wait_ge(s_te.h, st["h3_mm"])
                for i in range(NCH):
                    if gate_i:
                        TE.wait_ge(s_ve.h, st["apply_g"][i])
                    ins = TE.matmul(acc[:], wsl[:, i, 128 * jj:128 * (jj + 1)],
                                    rhs_of(i), start=(i == 0), stop=(i == NCH - 1))
                mk = s_te.inc(ins)
                if jj == 1:
                    done_wquarter(widx, mk)
                VE.wait_ge(s_te.h, mk)
                st[ev_key] = s_ve.inc(VE.tensor_copy(dst_of(j), acc[:]))
                acc_free[j % nacc[0]] = (s_ve, st[ev_key])

            def emit_h3_j(cw, j):
                widx = 12 + 16 * cw + 12 + j // 2
                wsl = use_wquarter(widx)
                jj = j % 2
                acc = accs[j % nacc[0]]
                wait_mark(TE, acc_free[j % nacc[0]])
                if j == 0:
                    TE.wait_ge(s_ve.h, st["h2_ev"])
                for i in range(NCH):
                    ins = TE.matmul(acc[:], wsl[:, i, 128 * jj:128 * (jj + 1)],
                                    h2st[:, 512 * i:512 * (i + 1)],
                                    start=(i == 0), stop=(i == NCH - 1))
                mk = s_te.inc(ins)
                st["h3_mm"] = mk
                if jj == 1:
                    done_wquarter(widx, mk)
                VE.wait_ge(s_te.h, mk)
                st["h3_cp"] = s_ve.inc(VE.tensor_copy(xl2[:, 512 * j:512 * (j + 1)], acc[:]))
                acc_free[j % nacc[0]] = (s_ve, st["h3_cp"])

            def emit_finish_j(cw, j):
                SE.wait_ge(s_ve.h, st["h3_cp"])
                st["tanh"] = s_se.inc(SE.activation(
                    xl2[:, 512 * j:512 * (j + 1)], xl2[:, 512 * j:512 * (j + 1)],
                    AF.Tanh, bias=bias_pp[:, j:j + 1], scale=1.0))
                VE.wait_ge(s_se.h, st["tanh"])
                xs = xr_j[:, j, 512 * cw:512 * (cw + 1)]
                st["add"] = s_ve.inc(VE.tensor_add(xs, xs, xl2[:, 512 * j:512 * (j + 1)]))
                SP.wait_ge(s_ve.h, st["add"])
                for h4 in range(4):
                    off = N * j + 512 * cw + 128 * h4
                    src = bass.AP(xr, off, [[NCH * N, 128], [1, 128]])
                    dst = bass.AP(out_d.tensor, off, [[NCH * N, 128], [1, 128]])
                    st_sems[h4].inc(SP.dma_start(dst, src))

            def tail_thunks(cw, flush=False):
                th = []
                if flush:
                    for j in range(NCH):
                        th.append(lambda j=j: emit_stats1_j(cw, j))
                    th.append(lambda: emit_stats1_fin(cw))
                else:
                    for which in range(2):
                        for j in range(NCH):
                            th.append(lambda w=which, j=j: emit_stats_j(cw, w, j))
                        th.append(lambda w=which: emit_stats_fin(cw, w))
                th.append(lambda: emit_rstd(cw))
                for jp in ((0, 1), (2, 3), (4, 5), (6, 7)):
                    th.append(lambda jp=jp: emit_apply(cw, list(jp)))
                for j in range(NCH):
                    th.append(lambda j=j: emit_proj_j(
                        cw, j, 4, lambda i: xl2[:, 512 * i:512 * (i + 1)],
                        lambda j2: qb_ap[:, 2048 * j2 + 512 * cw:
                                         2048 * j2 + 512 * cw + 512], "h1_ev",
                        gate_i=(flush and j == 0)))
                for j in range(NCH):
                    th.append(lambda j=j: emit_proj_j(
                        cw, j, 8,
                        lambda i: qb_ap[:, 2048 * i + 512 * cw:
                                        2048 * i + 512 * cw + 512],
                        lambda j2: h2st[:, 512 * j2:512 * (j2 + 1)], "h2_ev"))
                for j in range(NCH):
                    th.append(lambda j=j: emit_h3_j(cw, j))
                    th.append(lambda j=j: emit_finish_j(cw, j))
                return th

            ao_ready = {}
            work_q = []
            wp_q = []
            work_q_len = [1]
            work_q_done = [0]
            pending_norm = []
            TE.wait_ge(s_ve.h, s_ve.n)
            SE.wait_ge(s_ve.h, cbias_mark)
            emit_qk(0)
            emit_qk(1)
            for t, (c, p, m) in enumerate(iters):
                sl = m % 2
                SE.wait_ge(s_te.h, qk_mark.pop(t))
                if den_slot[sl]:
                    SE.wait_ge(s_te.h, den_slot[sl])
                ins = SE.activation(p_un[sl], scs[sl][:], AF.Exp,
                                    bias=cbias[:], scale=0.125)
                exp_slot[sl] = s_se.inc(ins)
                if t + 2 < len(iters):
                    emit_qk(t + 2)
                    if t + 2 == len(iters) - 2 * NM - 1:
                        preload_flush_kb(qk_mark[t + 2])
                TE.wait_ge(s_se.h, exp_slot[sl])
                if m == 0 and pv_evac_mark:
                    TE.wait_ge(s_ve.h, pv_evac_mark)
                if m == 0 and recip_mark:
                    TE.wait_ge(s_ve.h, recip_mark)
                TE.matmul(pvs[0:64, :],
                          vt[:, 1024 * m + 128 * p: 1024 * m + 128 * p + 64],
                          p_un[sl][:, 0:512],
                          start=(m == 0), stop=(m == NM - 1), tile_position=(0, 0),
                          skip_group_check=True)
                TE.matmul(pvs[64:128, :],
                          vt[:, 1024 * m + 128 * p + 64: 1024 * m + 128 * (p + 1)],
                          p_un[sl][:, 512:1024],
                          start=(m == 0), stop=(m == NM - 1), tile_position=(0, 64),
                          skip_group_check=True)
                TE.matmul(dns[0:64, :], ones64[:], p_un[sl][:, 0:512],
                          start=(m == 0), stop=(m == NM - 1), tile_position=(0, 0),
                          skip_group_check=True)
                ins = TE.matmul(dns[64:128, :], ones64[:], p_un[sl][:, 512:1024],
                                start=(m == 0), stop=(m == NM - 1), tile_position=(0, 64),
                                skip_group_check=True)
                den_slot[sl] = s_te.inc(ins)
                if m == NM - 1:
                    VE.wait_ge(s_te.h, den_slot[sl])
                    if ao_mult_mark:
                        VE.wait_ge(s_ve.h, ao_mult_mark)
                    pv_evac_mark = s_ve.inc(VE.tensor_copy(ao_un, pvs[:]))
                    VE.wait_ge(s_ve.h, pv_evac_mark)
                    dn_cp = s_ve.inc(VE.tensor_copy(dr, dns[:]))
                    recip_mark = dn_cp

                    def _norm(cp=c, pp=p, dncp=dn_cp):
                        SE.wait_ge(s_ve.h, dncp)
                        m1 = s_se.inc(SE.activation(dr, dr, AF.Ln, bias=0.0, scale=1.0))
                        SE.wait_ge(s_se.h, m1)
                        m1 = s_se.inc(SE.activation(dr, dr, AF.Exp, bias=0.0, scale=-1.0))
                        VE.wait_ge(s_se.h, m1)
                        while wp_q:
                            wp_q.pop(0)()
                        if cp > 0:
                            VE.wait_ge(s_te.h, st["wp_mm"])
                        mm = s_ve.inc(VE.tensor_mul(ao[:, 512 * pp:512 * (pp + 1)],
                                                    ao_un, dr))
                        if pp == NCH - 1:
                            ao_ready[cp] = mm
                            wp_q.extend(lambda j=j2: emit_wp_j(cp, j) for j2 in range(NCH))
                            work_q_done[0] = 0
                            work_q.extend(tail_thunks(cp, flush=(cp == NCC - 1)))
                            work_q_len[0] = len(work_q)
                        return mm

                    pending_norm.append(_norm)
                if pending_norm and m == 1:
                    ao_mult_mark = pending_norm.pop(0)()
                if wp_q and m in (2, 4, 6, 8, 10, 12, 14, 0):
                    wp_q.pop(0)()
                it_in_c = NM * p + m
                budget = (it_in_c + 1) * work_q_len[0] // (NM * NCH)
                while work_q and work_q_done[0] < budget:
                    work_q.pop(0)()
                    work_q_done[0] += 1
            preload_flush_vt()
            while pending_norm:
                ao_mult_mark = pending_norm.pop(0)()
            accs.extend([pvs, dns])
            acc_free.extend([(s_ve, ao_mult_mark), (s_ve, ao_mult_mark)])
            nacc[0] = 4
            while wp_q:
                wp_q.pop(0)()
            while work_q:
                work_q.pop(0)()
        for sm in (d_o[0], d_o[1], d_x[0], d_x[1]):
            SP.wait_ge(sm.h, sm.n)
        nc.all_engine_barrier()
    return nc


def _finish_dbg(nc, out_d, xr, dbg_d=None, tensors=None, tiny=None):
    SP = nc.sync
    nc.all_engine_barrier()
    cnt = 0
    with nc.semaphore("dbg") as dbg:
        SP.dma_start(out_d, xr[:]).then_inc(dbg, 16)
        cnt += 16
        if dbg_d:
            qb, kb, vt, S, cbias = tensors
            for nm, t in (("dbg_q", qb), ("dbg_k", kb), ("dbg_s", S), ("dbg_v", vt)):
                SP.dma_start(dbg_d[nm], t[:]).then_inc(dbg, 16)
                cnt += 16
            SP.dma_start(dbg_d["dbg_c"], cbias[:]).then_inc(dbg, 16)
            cnt += 16
        SP.wait_ge(dbg, cnt)
    nc.all_engine_barrier()


_NC_CACHE = {}


def _get_nc():
    if "nc" not in _NC_CACHE:
        _NC_CACHE["nc"] = build_program()
    return _NC_CACHE["nc"]


def kernel(L0_attn_x, L0_attn_Wq, L0_attn_Wk, L0_attn_Wv, L0_attn_Wp,
           L0_ffn_weight, L0_ffn_aux_weight1, L0_ffn_aux_weight2, L0_ffn_bias):
    x = np.ascontiguousarray(np.asarray(L0_attn_x, dtype=np.float32))
    ws = {
        "wq": np.ascontiguousarray(np.asarray(L0_attn_Wq, dtype=np.float32)),
        "wk": np.ascontiguousarray(np.asarray(L0_attn_Wk, dtype=np.float32)),
        "wv": np.ascontiguousarray(np.asarray(L0_attn_Wv, dtype=np.float32)),
        "wp": np.ascontiguousarray(np.asarray(L0_attn_Wp, dtype=np.float32)),
        "w1": np.ascontiguousarray(np.asarray(L0_ffn_weight, dtype=np.float32)),
        "w2": np.ascontiguousarray(np.asarray(L0_ffn_aux_weight1, dtype=np.float32)),
        "w3": np.ascontiguousarray(np.asarray(L0_ffn_aux_weight2, dtype=np.float32)),
    }
    bias = np.ascontiguousarray(np.asarray(L0_ffn_bias, dtype=np.float32))
    identb = np.eye(128, dtype=ml_dtypes.bfloat16)
    identf = np.eye(128, dtype=np.float32)

    nc = _get_nc()
    in_maps = []
    for b in range(8):
        im = {"x": np.ascontiguousarray(x[b].T), "bias": bias,
              "identb": identb, "identf": identf}
        im.update(ws)
        in_maps.append(im)
    res = run_bass_kernel_spmd(nc, in_maps, core_ids=list(range(8)))
    outs = []
    for r in res.results:
        o = r["out"].reshape(128, NCH, N).transpose(1, 0, 2).reshape(D, N)
        outs.append(o.T)
    return np.stack(outs, axis=0)


# version 36
# speedup vs baseline: 1.0097x; 1.0097x over previous
import contextlib
import numpy as np
import ml_dtypes

import concourse.bass as bass
import concourse.mybir as mybir
from concourse.bass_utils import run_bass_kernel_spmd

F32 = mybir.dt.float32
BF16 = mybir.dt.bfloat16
AF = mybir.ActivationFunctionType
ALU = mybir.AluOpType
AX = mybir.AxisListType

N = 2048
D = 1024
NT = N // 128
NCH = D // 128
NM = N // 128
CW = 512
NCC = N // CW
EPS = 1e-5
NWS = 2
FLUSH_W0 = 62
FL_SEM = [0, 1, 2, 3, 4, 4, 5, 5, 5, 5, 5, 5, 5, 5]


class ESem:

    def __init__(self, handle):
        self.h = handle
        self.n = 0

    def inc(self, instr):
        instr.then_inc(self.h, 1)
        self.n += 1
        return self.n


class DSem:

    def __init__(self, handle):
        self.h = handle
        self.n = 0

    def inc(self, instr):
        instr.then_inc(self.h, 16)
        self.n += 16
        return self.n


def build_program(upto=99, dbg=False):
    nc = bass.Bass("TRN2", target_bir_lowering=False)

    x_d = nc.dram_tensor("x", [D, N], F32, kind="ExternalInput").ap()
    w_d = {}
    for wn in ("wq", "wk", "wv", "wp", "w1", "w2", "w3"):
        w_d[wn] = nc.dram_tensor(wn, [D, D], F32, kind="ExternalInput").ap()
    bias_d = nc.dram_tensor("bias", [D], F32, kind="ExternalInput").ap()
    identb_d = nc.dram_tensor("identb", [128, 128], BF16, kind="ExternalInput").ap()
    identf_d = nc.dram_tensor("identf", [128, 128], F32, kind="ExternalInput").ap()
    out_d = nc.dram_tensor("out", [128, NCH * N], F32, kind="ExternalOutput").ap()
    dbg_d = {}
    if dbg:
        for nm in ("dbg_q", "dbg_k", "dbg_s"):
            dbg_d[nm] = nc.dram_tensor(nm, [128, NCH * N], BF16, kind="ExternalOutput").ap()
        dbg_d["dbg_v"] = nc.dram_tensor("dbg_v", [128, NT * D], BF16, kind="ExternalOutput").ap()
        dbg_d["dbg_c"] = nc.dram_tensor("dbg_c", [128, 1], F32, kind="ExternalOutput").ap()

    ctx = contextlib.ExitStack()
    E = ctx.enter_context
    with ctx:
        xr = E(nc.sbuf_tensor("xr", [128, NCH * N], F32))
        S = E(nc.sbuf_tensor("S", [128, NCH * N], BF16))
        qb = E(nc.sbuf_tensor("qb", [128, NCH * N], BF16))
        kb = E(nc.sbuf_tensor("kb", [128, NCH * N], BF16))
        vt = E(nc.sbuf_tensor("vt", [128, NT * D], BF16))
        wbuf = E(nc.sbuf_tensor("wbuf", [128, NWS * 8 * 256], BF16))
        vtmp = E(nc.sbuf_tensor("vtmp", [128, 3072], BF16))
        identb = E(nc.sbuf_tensor("identb_sb", [128, 128], BF16))
        identf = E(nc.sbuf_tensor("identf_sb", [128, 8], F32))
        ones64 = E(nc.sbuf_tensor("ones64", [128, 64], BF16))
        ones128 = E(nc.sbuf_tensor("ones128", [128, 128], BF16))
        bias_pp = E(nc.sbuf_tensor("bias_pp_sb", [128, 8], F32))
        cbias = E(nc.sbuf_tensor("cbias", [128, 1], F32))
        tiny = E(nc.sbuf_tensor("tiny", [128, 8], F32))
        bias_st = E(nc.sbuf_tensor("bias_st", [8, 128], F32))

        S_ap = S.ap()
        qb_ap = qb.ap()
        kb_ap = kb.ap()
        xr_ap = xr.ap()
        Sf = S_ap.bitcast(F32)
        qbf = qb_ap.bitcast(F32)
        kbf = kb_ap.bitcast(F32)
        vtf = vt.ap().bitcast(F32)
        xr_j = xr_ap.rearrange("p (j n) -> p j n", j=NCH)
        vt_t = vt.ap().rearrange("p (t d) -> p t d", t=NT)

        s_te = ESem(E(nc.semaphore("s_te")))
        s_gp = ESem(E(nc.semaphore("s_gp")))
        gp_mark = [0]
        s_ve = ESem(E(nc.semaphore("s_ve")))
        s_se = ESem(E(nc.semaphore("s_se")))
        d_x = [DSem(E(nc.semaphore(f"d_x{i}"))) for i in range(4)]
        d_w = [DSem(E(nc.semaphore(f"d_w{i}"))) for i in range(NWS)]
        d_wf = [DSem(E(nc.semaphore(f"d_wf{i}"))) for i in range(6)]
        d_wh = [DSem(E(nc.semaphore(f"d_wh{i}"))) for i in range(4)]
        WH_SEM = [0, 1, 2, 3, 2, 3, 2, 3]
        d_o = [DSem(E(nc.semaphore(f"d_o{i}"))) for i in range(2)]
        d_misc = DSem(E(nc.semaphore("d_misc")))

        TE, VE, SE, SP, GP = nc.tensor, nc.vector, nc.scalar, nc.sync, nc.gpsimd
        _uid = [0]

        def _nm(base):
            _uid[0] += 1
            return f"{base}_{_uid[0]}"

        wq_order = []
        for wn in ("wq", "wk", "wv"):
            wq_order += [(wn, q) for q in range(4)]
        for _ in range(NCC):
            for wn in ("wp", "w1", "w2", "w3"):
                wq_order += [(wn, q) for q in range(4)]

        wslot_free = [0] * NWS

        def wslot(s):
            return wbuf[:, 2048 * s: 2048 * (s + 1)].rearrange("p (i d) -> p i d", i=8)

        def issue_wdma(idx):
            if idx >= FLUSH_W0 or idx >= len(wq_order):
                return
            wn, q = wq_order[idx]
            s = idx % NWS
            GP.wait_ge(s_te.h, wslot_free[s])
            src = w_d[wn].rearrange("(i p) d -> p i d", p=128)[:, :, 256 * q: 256 * (q + 1)]
            d_w[s].inc(GP.dma_start(wslot(s)[:], src))

        def vtslot(k):
            return vt.ap()[:, 2048 * k: 2048 * (k + 1)].rearrange("p (i d) -> p i d", i=8)

        def kbslot(k):
            return kb_ap[:, 2048 * k: 2048 * (k + 1)].rearrange("p (i d) -> p i d", i=8)

        def flslot(k):
            return kbslot(k) if k < 6 else vtslot(k - 6)

        def use_wquarter(idx):
            if idx < 8:
                s = WH_SEM[idx]
                nh = sum(1 for ss in WH_SEM if ss == s)
                TE.wait_ge(d_wh[s].h, 16 * nh)
                return vtslot(idx)
            if idx >= FLUSH_W0:
                k = idx - FLUSH_W0
                s = FL_SEM[k]
                nf = sum(1 for ss in FL_SEM if ss == s)
                TE.wait_ge(d_wf[s].h, 16 * nf)
                return flslot(k)
            s = idx % NWS
            nloads = sum(1 for i in range(8, idx + 1) if i % NWS == s)
            TE.wait_ge(d_w[s].h, 16 * nloads)
            return wslot(s)

        def done_wquarter(idx, te_mark):
            if idx < 8 or idx >= FLUSH_W0:
                return
            wslot_free[idx % NWS] = te_mark
            issue_wdma(idx + NWS)

        def preload_head_quarters():
            for k in range(8):
                wn, q = wq_order[k]
                src = w_d[wn].rearrange("(i p) d -> p i d",
                                        p=128)[:, :, 256 * q: 256 * (q + 1)]
                d_wh[WH_SEM[k]].inc(GP.dma_start(vtslot(k)[:], src))

        def _fl_load(k):
            wn, q = wq_order[FLUSH_W0 + k]
            src = w_d[wn].rearrange("(i p) d -> p i d",
                                    p=128)[:, :, 256 * q: 256 * (q + 1)]
            d_wf[FL_SEM[k]].inc(GP.dma_start(flslot(k)[:], src))

        def preload_flush_kb(gate_mark):
            GP.wait_ge(s_te.h, gate_mark)
            for k in range(6):
                _fl_load(k)

        def preload_flush_vt():
            GP.wait_ge(s_te.h, s_te.n)
            for k in range(6, len(wq_order) - FLUSH_W0):
                _fl_load(k)

        d_misc.inc(SP.dma_start(identb[:], identb_d))
        d_misc.inc(SP.dma_start(identf[:], identf_d[:, 0:8]))
        d_misc.inc(SP.dma_start(bias_st[:], bias_d.rearrange("(j p) -> j p", p=128)))
        VE.memset(ones64[:], 1.0)
        VE.memset(ones128[:], 1.0)
        VE.memset(tiny[:, 2:3], EPS)
        for e in (TE, VE, SE, SP):
            e.wait_ge(d_misc.h, d_misc.n)
        xsrc = x_d.rearrange("(jj q p) n -> jj p q n", p=128, q=2)
        for jj in range(4):
            d_x[jj].inc(SP.dma_start(xr_j[:, 2 * jj:2 * (jj + 1), :], xsrc[jj]))
        preload_head_quarters()
        issue_wdma(8)
        issue_wdma(9)
        with nc.psum_tensor("ps_bias", [128, 8], F32) as psb:
            mkb = s_te.inc(TE.transpose(psb[:], bias_st[:], identf[0:8, 0:8]))
            VE.wait_ge(s_te.h, mkb)
            mkb2 = s_ve.inc(VE.tensor_copy(bias_pp[:], psb[:]))
            TE.wait_ge(s_ve.h, mkb2)

        def layernorm(src_f32_j, dst_bf16, add_back, dma_gate=False, wq0=False):
            xbf = [qb_ap[:, 0:2048], qb_ap[:, 2048:4096]]
            xsq = [qb_ap[:, 4096:6144], qb_ap[:, 6144:8192]]
            mean_sb = qbf[:, 4096:6144]
            rstd_sb = qbf[:, 6144:8192]
            tmpf = qbf[:, 0:2048]
            tmpf2pp = [qbf[:, 2048:4096], kbf[:, 0:2048]]
            cast_mark = [0, 0]
            with nc.psum_tensor(_nm("ln_mean"), [128, 2048], F32) as mps, \
                 nc.psum_tensor(_nm("ln_sq"), [128, 2048], F32) as sps:
                mm_mark = [0, 0]
                for j in range(NCH):
                    sl = j % 2
                    if dma_gate:
                        VE.wait_ge(d_x[j // 2].h, 16)
                    if j >= 2:
                        VE.wait_ge(s_te.h, mm_mark[sl])
                    m1 = s_ve.inc(VE.tensor_copy(xbf[sl], src_f32_j[:, j, :]))
                    VE.wait_ge(s_ve.h, m1)
                    m2 = s_ve.inc(VE.tensor_mul(xsq[sl], xbf[sl], xbf[sl]))
                    TE.wait_ge(s_ve.h, m2)
                    for n4 in range(4):
                        TE.matmul(mps[:, 512 * n4:512 * (n4 + 1)], ones128[:],
                                  xbf[sl][:, 512 * n4:512 * (n4 + 1)],
                                  start=(j == 0), stop=(j == NCH - 1))
                        ins = TE.matmul(sps[:, 512 * n4:512 * (n4 + 1)], ones128[:],
                                        xsq[sl][:, 512 * n4:512 * (n4 + 1)],
                                        start=(j == 0), stop=(j == NCH - 1))
                    mm_mark[sl] = s_te.inc(ins)
                VE.wait_ge(s_te.h, s_te.n)
                m = s_ve.inc(VE.tensor_scalar(mean_sb, mps[:], 1.0 / D, None, ALU.mult))
                VE.wait_ge(s_ve.h, m)
                m = s_ve.inc(VE.tensor_mul(rstd_sb, mean_sb, mean_sb))
                VE.wait_ge(s_ve.h, m)
                m = s_ve.inc(VE.scalar_tensor_tensor(rstd_sb, sps[:], 1.0 / D, rstd_sb,
                                                     ALU.mult, ALU.subtract))
                SE.wait_ge(s_ve.h, m)
                m = s_se.inc(SE.activation(rstd_sb, rstd_sb, AF.Ln, bias=tiny[:, 2:3], scale=1.0))
                SE.wait_ge(s_se.h, m)
                rstd_mark = s_se.inc(SE.activation(rstd_sb, rstd_sb, AF.Exp,
                                                   bias=0.0, scale=-0.5))
            def _normalize(wq0_pj):
                wsl0 = use_wquarter(0) if wq0_pj is not None else None
                mkq0 = 0
                deferred = []
                for j in range(NCH):
                    src_jj = src_f32_j[:, j, :]
                    dst_jj = dst_bf16[:, 2048 * j:2048 * (j + 1)]
                    m = s_ve.inc(VE.tensor_sub(tmpf, src_jj, mean_sb))
                    VE.wait_ge(s_ve.h, m)
                    if j == 0:
                        VE.wait_ge(s_se.h, rstd_mark)
                    if add_back:
                        sl2 = j % 2
                        if j >= 5:
                            buf = kbf[:, 2048 * (j - 4):2048 * (j - 3)]
                            m = s_ve.inc(VE.tensor_mul(buf, tmpf, rstd_sb))
                            SE.wait_ge(s_ve.h, m)
                            cast_mark[sl2] = s_se.inc(SE.copy(dst_jj, buf))
                            VE.wait_ge(s_ve.h, m)
                            deferred.append((src_jj, buf))
                        else:
                            if cast_mark[sl2]:
                                VE.wait_ge(s_se.h, cast_mark[sl2])
                            m = s_ve.inc(VE.tensor_mul(tmpf2pp[sl2], tmpf, rstd_sb))
                            SE.wait_ge(s_ve.h, m)
                            cast_mark[sl2] = s_se.inc(SE.copy(dst_jj, tmpf2pp[sl2]))
                            VE.wait_ge(s_ve.h, m)
                            m = s_ve.inc(VE.tensor_add(src_jj, src_jj, tmpf2pp[sl2]))
                            VE.wait_ge(s_ve.h, m)
                        if wq0_pj is not None:
                            TE.wait_ge(s_se.h, cast_mark[sl2])
                            for half in range(2):
                                for n4 in range(4):
                                    ins = TE.matmul(
                                        wq0_pj[half][:, 512 * n4:512 * (n4 + 1)],
                                        wsl0[:, j, 128 * half:128 * (half + 1)],
                                        dst_bf16[:, 2048 * j + 512 * n4:
                                                 2048 * j + 512 * (n4 + 1)],
                                        start=(j == 0), stop=(j == NCH - 1),
                                        skip_group_check=True)
                            mkq0 = s_te.inc(ins)
                    else:
                        m = s_ve.inc(VE.tensor_mul(dst_jj, tmpf, rstd_sb))
                        VE.wait_ge(s_ve.h, m)
                if wq0_pj is None:
                    nc.all_engine_barrier()
                    return None
                SE.wait_ge(s_ve.h, s_ve.n)
                SE.wait_ge(s_te.h, mkq0)
                ev0 = s_se.inc(SE.copy(qb_ap[:, 0:2048], wq0_pj[0][:]))
                ev1 = s_se.inc(SE.copy(qb_ap[:, 2048:4096], wq0_pj[1][:]))
                add_done = 0
                for src_jj, buf in deferred:
                    add_done = s_ve.inc(VE.tensor_add(src_jj, src_jj, buf))
                    VE.wait_ge(s_ve.h, add_done)
                return (ev0, ev1, add_done)

            if wq0:
                with nc.psum_tensor(_nm("pq0a"), [128, 2048], F32) as pq0a, \
                     nc.psum_tensor(_nm("pq0b"), [128, 2048], F32) as pq0b:
                    return _normalize([pq0a, pq0b])
            return _normalize(None)

        ln_evs = layernorm(xr_j, S_ap, add_back=True, dma_gate=True, wq0=True)
        if upto <= 1:
            _finish_dbg(nc, out_d, xr, dbg_d, [qb, kb, vt, S, cbias], tiny)
            return nc

        def wproj(widx0, src_bf, dst_bf, q0=0, ev_init=None):
            with nc.psum_tensor(_nm("pj0"), [128, 2048], F32) as p0, \
                 nc.psum_tensor(_nm("pj1"), [128, 2048], F32) as p1:
                pj = [p0, p1]
                ev_mark = [0, 0]
                first = True
                for q in range(q0, 4):
                    widx = widx0 + q
                    wsl = use_wquarter(widx)
                    for jj in range(2):
                        j = 2 * q + jj
                        sl = j % 2
                        if first and ev_init is not None:
                            TE.wait_ge(s_se.h, max(ev_init))
                        first = False
                        if ev_mark[sl]:
                            TE.wait_ge(s_se.h, ev_mark[sl])
                        for i in range(NCH):
                            for n4 in range(4):
                                ins = TE.matmul(
                                    pj[sl][:, 512 * n4:512 * (n4 + 1)],
                                    wsl[:, i, 128 * jj:128 * (jj + 1)],
                                    src_bf[:, 2048 * i + 512 * n4: 2048 * i + 512 * (n4 + 1)],
                                    start=(i == 0), stop=(i == NCH - 1))
                        mk = s_te.inc(ins)
                        SE.wait_ge(s_te.h, mk)
                        ev_mark[sl] = s_se.inc(SE.copy(dst_bf[:, 2048 * j:2048 * (j + 1)],
                                                       pj[sl][:]))
                        if jj == 1:
                            done_wquarter(widx, mk)
            nc.all_engine_barrier()

        wproj(0, S_ap, qb_ap, q0=1, ev_init=ln_evs[:2])

        rmax = tiny[:, 0:2]
        VE.memset(rmax, 0.0)
        qsq_pp = [vtmp[:, 0:1024], vtmp[:, 1024:2048]]
        with nc.psum_tensor("pk0", [128, 1024], F32) as pk0, \
             nc.psum_tensor("pk1", [128, 1024], F32) as pk1, \
             nc.psum_tensor("cq0", [128, 1024], F32) as cq0, \
             nc.psum_tensor("cq1", [128, 1024], F32) as cq1:
            pj = [pk0, pk1]
            cbq = [cq0, cq1]
            ev_mark = [0, 0]
            q4_sq = [0, 0]
            q4_mm = [0, 0]
            q4_red = [0, 0]
            q4_rm = [0]
            q4_next = [0]

            def p4q_sub():
                idx = q4_next[0]
                if idx >= 18:
                    return
                q4_next[0] += 1
                if idx < 16:
                    p, hf = divmod(idx, 2)
                    sl = idx % 2
                    if q4_mm[sl]:
                        SE.wait_ge(s_te.h, q4_mm[sl])
                    q4_sq[sl] = s_se.inc(SE.activation(
                        qsq_pp[sl], qb_ap[:, 2048 * p + 1024 * hf:
                                          2048 * p + 1024 * (hf + 1)],
                        AF.Square, bias=0.0, scale=1.0))
                if 0 <= idx - 1 < 16:
                    sl = (idx - 1) % 2
                    TE.wait_ge(s_se.h, q4_sq[sl])
                    if q4_red[sl]:
                        TE.wait_ge(s_ve.h, q4_red[sl])
                    for n2 in range(2):
                        ins = TE.matmul(cbq[sl][:, 512 * n2:512 * (n2 + 1)],
                                        ones128[:],
                                        qsq_pp[sl][:, 512 * n2:512 * (n2 + 1)],
                                        start=True, stop=True,
                                        skip_group_check=True)
                    q4_mm[sl] = s_te.inc(ins)
                if idx - 2 >= 0:
                    sl = (idx - 2) % 2
                    VE.wait_ge(s_te.h, q4_mm[sl])
                    if q4_rm[0]:
                        VE.wait_ge(s_ve.h, q4_rm[0])
                    m = s_ve.inc(VE.reduce_max(tiny[:, 6:7], cbq[sl][:], axis=AX.X))
                    VE.wait_ge(s_ve.h, m)
                    q4_rm[0] = s_ve.inc(VE.tensor_max(rmax[:, 0:1], rmax[:, 0:1],
                                                      tiny[:, 6:7]))
                    q4_red[sl] = q4_rm[0]

            for q in range(4):
                widx = 4 + q
                wsl = use_wquarter(widx)
                for jj in range(2):
                    j = 2 * q + jj
                    for hf in range(2):
                        sl = hf
                        if ev_mark[sl]:
                            TE.wait_ge(s_se.h, ev_mark[sl])
                        for i in range(NCH):
                            for n2 in range(2):
                                ins = TE.matmul(
                                    pj[sl][:, 512 * n2:512 * (n2 + 1)],
                                    wsl[:, i, 128 * jj:128 * (jj + 1)],
                                    S_ap[:, 2048 * i + 1024 * hf + 512 * n2:
                                         2048 * i + 1024 * hf + 512 * (n2 + 1)],
                                    start=(i == 0), stop=(i == NCH - 1),
                                    skip_group_check=True)
                        mk = s_te.inc(ins)
                        SE.wait_ge(s_te.h, mk)
                        if j == 0 and hf == 0 and ln_evs[2]:
                            SE.wait_ge(s_ve.h, ln_evs[2])
                        ev_mark[sl] = s_se.inc(SE.copy(
                            kb_ap[:, 2048 * j + 1024 * hf:
                                  2048 * j + 1024 * (hf + 1)], pj[sl][:]))
                        if jj == 1 and hf == 1:
                            done_wquarter(widx, mk)
                        p4q_sub()
                        p4q_sub()
            while q4_next[0] < 18:
                p4q_sub()
        nc.all_engine_barrier()
        if upto <= 2:
            _finish_dbg(nc, out_d, xr, dbg_d, [qb, kb, vt, S, cbias], tiny)
            return nc

        with nc.psum_tensor("pva", [128, 1024], F32) as pva, \
             nc.psum_tensor("pvb", [128, 1024], F32) as pvb, \
             nc.psum_tensor("tp0", [128, 512], BF16) as tp0, \
             nc.psum_tensor("tp1", [128, 512], BF16) as tp1, \
             nc.psum_tensor("tp2", [128, 512], BF16) as tp2, \
             nc.psum_tensor("tp3", [128, 512], BF16) as tp3:
            pvh = [pva, pvb]
            tps = [tp0, tp1, tp2, tp3]
            tp_mark = [0, 0, 0, 0]
            pv_mk = [0, 0]
            pv_evac_h = [0, 0]
            vtmp_half_done = [0, 0]
            for q in range(4):
                widx = 8 + q
                wsl = use_wquarter(widx)
                for jj in range(2):
                    j = 2 * q + jj
                    for h in range(2):
                        if pv_evac_h[h]:
                            TE.wait_ge(s_ve.h, pv_evac_h[h])
                        for i in range(NCH):
                            for n2 in range(2):
                                ins = TE.matmul(
                                    pvh[h][:, 512 * n2:512 * (n2 + 1)],
                                    wsl[:, i, 128 * jj:128 * (jj + 1)],
                                    S_ap[:, 2048 * i + 1024 * h + 512 * n2:
                                         2048 * i + 1024 * h + 512 * (n2 + 1)],
                                    start=(i == 0), stop=(i == NCH - 1),
                                    skip_group_check=True)
                        pv_mk[h] = s_te.inc(ins)
                        if jj == 1 and h == 1:
                            done_wquarter(widx, pv_mk[h])
                        VE.wait_ge(s_te.h, pv_mk[h])
                        if vtmp_half_done[h]:
                            VE.wait_ge(s_te.h, vtmp_half_done[h])
                        pv_evac_h[h] = s_ve.inc(
                            VE.tensor_copy(vtmp[:, 1024 * h:1024 * (h + 1)], pvh[h][:]))
                    for h in range(2):
                        TE.wait_ge(s_ve.h, pv_evac_h[h])
                        for sg2 in range(2):
                            sgrp = 2 * h + sg2
                            sl = sgrp % 4
                            if tp_mark[sl]:
                                TE.wait_ge(s_ve.h, tp_mark[sl])
                            for b in range(4):
                                tt = 4 * sgrp + b
                                ins = TE.transpose(tps[sl][:, 128 * b:128 * (b + 1)],
                                                   vtmp[:, 128 * tt:128 * (tt + 1)], identb[:])
                            mk2 = s_te.inc(ins)
                            if sg2 == 1:
                                vtmp_half_done[h] = mk2
                            VE.wait_ge(s_te.h, mk2)
                            tp_mark[sl] = s_ve.inc(
                                VE.tensor_copy(vt_t[:, 4 * sgrp:4 * (sgrp + 1),
                                                    128 * j:128 * (j + 1)],
                                               tps[sl][:]))
        if upto <= 3:
            nc.all_engine_barrier()
            _finish_dbg(nc, out_d, xr, dbg_d, [qb, kb, vt, S, cbias], tiny)
            return nc

        qsqs = [S_ap[:, 0:2048], S_ap[:, 2048:4096]]
        SE.wait_ge(s_te.h, s_te.n)
        TE.wait_ge(s_ve.h, s_ve.n)
        with nc.psum_tensor("cb0", [128, 2048], F32) as cb0, \
             nc.psum_tensor("cb1", [128, 2048], F32) as cb1:
            cbs = [cb0, cb1]
            p4_mm = [0, 0]
            p4_sq = [0, 0]
            p4_red = [0, 0]
            p4_rm = [0]

            def p4_square(step):
                sl = step % 2
                if p4_mm[sl]:
                    SE.wait_ge(s_te.h, p4_mm[sl])
                p4_sq[sl] = s_se.inc(SE.activation(
                    qsqs[sl], kb_ap[:, 2048 * step:2048 * (step + 1)], AF.Square,
                    bias=0.0, scale=1.0))

            def p4_mme(step):
                sl = step % 2
                TE.wait_ge(s_se.h, p4_sq[sl])
                if p4_red[sl]:
                    TE.wait_ge(s_ve.h, p4_red[sl])
                for n4 in range(4):
                    ins = TE.matmul(cbs[sl][:, 512 * n4:512 * (n4 + 1)], ones128[:],
                                    qsqs[sl][:, 512 * n4:512 * (n4 + 1)],
                                    start=True, stop=True)
                p4_mm[sl] = s_te.inc(ins)

            def p4_reduce(step):
                sl = step % 2
                VE.wait_ge(s_te.h, p4_mm[sl])
                if p4_rm[0]:
                    VE.wait_ge(s_ve.h, p4_rm[0])
                m = s_ve.inc(VE.reduce_max(tiny[:, 6:7], cbs[sl][:], axis=AX.X))
                VE.wait_ge(s_ve.h, m)
                p4_rm[0] = s_ve.inc(VE.tensor_max(rmax[:, 1:2],
                                                  rmax[:, 1:2], tiny[:, 6:7]))
                p4_red[sl] = p4_rm[0]

            for step in range(10):
                if step < NCH:
                    p4_square(step)
                if 1 <= step <= NCH:
                    p4_mme(step - 1)
                if step >= 2:
                    p4_reduce(step - 2)
            VE.wait_ge(s_ve.h, p4_rm[0])
            m = s_ve.inc(VE.tensor_add(tiny[:, 1:2], rmax[:, 0:1], rmax[:, 1:2]))
            VE.wait_ge(s_ve.h, m)
            cbias_mark = s_ve.inc(VE.tensor_scalar(cbias[:], tiny[:, 1:2],
                                                   -1.0 / 16.0, None, ALU.mult))
        if upto <= 4:
            nc.all_engine_barrier()
            _finish_dbg(nc, out_d, xr, dbg_d, [qb, kb, vt, S, cbias], tiny)
            return nc

        p_un = [S_ap[:, 0:1024], S_ap[:, 1024:2048]]
        ao = S_ap[:, 2048:6144]
        dr = Sf[:, 3072:3584]
        ao_un = Sf[:, 3584:4096]
        h2st = S_ap[:, 8192:12288]
        xl2 = S_ap[:, 12288:16384]
        vtmpf = vtmp.ap().bitcast(F32)
        xbf = vtmp[:, 0:512]
        xsq = vtmp[:, 512:1024]
        mean_c = vtmpf[:, 512:1024]
        rstd_c = vtmpf[:, 1024:1536]
        ostage = [vtmpf[:, 0:512], vtmpf[:, 512:1024]]

        iters = [(c, p, m) for c in range(NCC) for p in range(NCH) for m in range(NM)]
        with nc.psum_tensor("sc0", [128, 1024], F32) as sc0, \
             nc.psum_tensor("sc1", [128, 1024], F32) as sc1, \
             nc.psum_tensor("pvs", [128, 512], F32) as pvs, \
             nc.psum_tensor("dns", [128, 512], F32) as dns, \
             nc.psum_tensor("wps", [128, 512], F32) as wps, \
             nc.psum_tensor("lnps", [128, 512], F32) as lnps:
            scs = [sc0, sc1]
            exp_slot = [0, 0]
            den_slot = [0, 0]
            qk_mark = {}
            pv_evac_mark = 0
            recip_mark = 0
            ao_mult_mark = 0
            st = {
                "wp_mm": 0,
                "wp_ev": 0,
                "wps_free": (s_ve, 0),
                "xbf_free": 0,
                "sq_done": [0, 0],
                "sq_mm": [0, 0],
                "mean_ev": 0,
                "var_ev": 0,
                "rstd": 0,
                "apply": 0,
                "apply_g": [0] * NCH,
                "h1_ev": 0,
                "h2_ev": 0,
                "h3_cp": 0,
                "h3_mm": 0,
                "tanh": 0,
                "add": 0,
                "lnps_free": (s_ve, 0),
                "out_cp": 0,
            }

            st_sems = [d_o[0], d_o[1], d_x[0], d_x[1]]

            def wait_mark(eng, pair):
                sem, mark = pair
                if mark:
                    eng.wait_ge(sem.h, mark)

            def emit_qk(t):
                c, p, m = iters[t]
                sl = m % 2
                if exp_slot[sl]:
                    TE.wait_ge(s_se.h, exp_slot[sl])
                TE.matmul(scs[sl][:, 0:512],
                          kb[0:64, 2048 * p + 128 * m: 2048 * p + 128 * (m + 1)],
                          qb[0:64, 2048 * p + 512 * c: 2048 * p + 512 * (c + 1)],
                          start=True, stop=True, tile_position=(0, 0))
                ins = TE.matmul(scs[sl][:, 512:1024],
                                kb[64:128, 2048 * p + 128 * m: 2048 * p + 128 * (m + 1)],
                                qb[64:128, 2048 * p + 512 * c: 2048 * p + 512 * (c + 1)],
                                start=True, stop=True, tile_position=(64, 0))
                qk_mark[t] = s_te.inc(ins)

            accs = [wps, lnps]
            acc_free = [(s_ve, 0), (s_ve, 0)]
            nacc = [2]

            def emit_wp_j(cw, j):
                widx = 12 + 16 * cw + j // 2
                wsl = use_wquarter(widx)
                jj = j % 2
                acc = accs[j % nacc[0]]
                wait_mark(TE, acc_free[j % nacc[0]])
                TE.wait_ge(s_ve.h, ao_ready[cw])
                for i in range(NCH):
                    ins = TE.matmul(acc[:], wsl[:, i, 128 * jj:128 * (jj + 1)],
                                    ao[:, 512 * i:512 * (i + 1)],
                                    start=(i == 0), stop=(i == NCH - 1))
                st["wp_mm"] = s_te.inc(ins)
                if jj == 1:
                    done_wquarter(widx, st["wp_mm"])
                VE.wait_ge(s_te.h, st["wp_mm"])
                xs = xr_j[:, j, 512 * cw:512 * (cw + 1)]
                st["wp_ev"] = s_ve.inc(VE.tensor_add(xs, xs, acc[:]))
                acc_free[j % nacc[0]] = (s_ve, st["wp_ev"])

            def emit_stats_j(cw, which, j):
                acc = accs[1]
                if j == 0 and which == 0:
                    while wp_q:
                        wp_q.pop(0)()
                if st["xbf_free"]:
                    VE.wait_ge(s_te.h, st["xbf_free"])
                if j == 0 and which == 0:
                    VE.wait_ge(s_ve.h, st["wp_ev"])
                mcast = s_ve.inc(VE.tensor_copy(xbf, xr_j[:, j, 512 * cw:512 * (cw + 1)]))
                src = xbf
                if which == 1:
                    VE.wait_ge(s_ve.h, mcast)
                    mcast = s_ve.inc(VE.tensor_mul(xsq, xbf, xbf))
                    src = xsq
                TE.wait_ge(s_ve.h, mcast)
                if j == 0:
                    wait_mark(TE, acc_free[1] if which == 0 else (s_ve, st["mean_ev"]))
                ins = TE.matmul(acc[:], ones128[:], src,
                                start=(j == 0), stop=(j == NCH - 1))
                st["xbf_free"] = s_te.inc(ins)

            def emit_stats1_j(cw, j):
                sl = j % 2
                xb = [vtmp[:, 0:512], vtmp[:, 512:1024]][sl]
                xq = [vtmp[:, 1024:1536], vtmp[:, 1536:2048]][sl]
                if j == 0:
                    while wp_q:
                        wp_q.pop(0)()
                    VE.wait_ge(s_ve.h, st["wp_ev"])
                if st["sq_mm"][sl]:
                    VE.wait_ge(s_te.h, st["sq_mm"][sl])
                if st["sq_done"][sl]:
                    VE.wait_ge(s_se.h, st["sq_done"][sl])
                mcast = s_ve.inc(VE.tensor_copy(xb, xr_j[:, j, 512 * cw:512 * (cw + 1)]))
                TE.wait_ge(s_ve.h, mcast)
                if j == 0:
                    wait_mark(TE, acc_free[1])
                    wait_mark(TE, acc_free[3])
                ins = TE.matmul(accs[1][:], ones128[:], xb,
                                start=(j == 0), stop=(j == NCH - 1))
                s_te.inc(ins)
                SE.wait_ge(s_ve.h, mcast)
                st["sq_done"][sl] = s_se.inc(SE.activation(xq, xb, AF.Square,
                                                           bias=0.0, scale=1.0))
                TE.wait_ge(s_se.h, st["sq_done"][sl])
                ins = TE.matmul(accs[3][:], ones128[:], xq,
                                start=(j == 0), stop=(j == NCH - 1))
                st["sq_mm"][sl] = s_te.inc(ins)
                st["xbf_free"] = st["sq_mm"][sl]

            def emit_stats1_fin(cw):
                VE.wait_ge(s_te.h, st["xbf_free"])
                st["mean_ev"] = s_ve.inc(VE.tensor_scalar(mean_c, lnps[:], 1.0 / D,
                                                          None, ALU.mult))
                VE.wait_ge(s_ve.h, st["mean_ev"])
                m1 = s_ve.inc(VE.tensor_mul(rstd_c, mean_c, mean_c))
                VE.wait_ge(s_ve.h, m1)
                st["var_ev"] = s_ve.inc(VE.scalar_tensor_tensor(
                    rstd_c, accs[3][:], 1.0 / D, rstd_c, ALU.mult, ALU.subtract))
                acc_free[1] = (s_ve, st["var_ev"])
                acc_free[3] = (s_ve, st["var_ev"])

            def emit_stats_fin(cw, which):
                VE.wait_ge(s_te.h, st["xbf_free"])
                if which == 0:
                    st["mean_ev"] = s_ve.inc(VE.tensor_scalar(mean_c, lnps[:], 1.0 / D,
                                                              None, ALU.mult))
                else:
                    VE.wait_ge(s_ve.h, st["mean_ev"])
                    m1 = s_ve.inc(VE.tensor_mul(rstd_c, mean_c, mean_c))
                    VE.wait_ge(s_ve.h, m1)
                    st["var_ev"] = s_ve.inc(VE.scalar_tensor_tensor(
                        rstd_c, lnps[:], 1.0 / D, rstd_c, ALU.mult, ALU.subtract))
                    acc_free[1] = (s_ve, st["var_ev"])

            def emit_rstd(cw):
                SE.wait_ge(s_ve.h, st["var_ev"])
                m1 = s_se.inc(SE.activation(rstd_c, rstd_c, AF.Ln,
                                            bias=tiny[:, 2:3], scale=1.0))
                SE.wait_ge(s_se.h, m1)
                st["rstd"] = s_se.inc(SE.activation(rstd_c, rstd_c, AF.Exp,
                                                    bias=0.0, scale=-0.5))

            def emit_apply(cw, jlist):
                tmpf32 = vtmpf[:, 0:512]
                for j in jlist:
                    VE.wait_ge(s_se.h, st["rstd"])
                    if st["xbf_free"]:
                        VE.wait_ge(s_te.h, st["xbf_free"])
                        st["xbf_free"] = 0
                    if st["apply"]:
                        VE.wait_ge(s_ve.h, st["apply"])
                    m1 = s_ve.inc(VE.tensor_sub(tmpf32, xr_j[:, j, 512 * cw:512 * (cw + 1)],
                                                mean_c))
                    VE.wait_ge(s_ve.h, m1)
                    st["apply"] = s_ve.inc(VE.tensor_mul(xl2[:, 512 * j:512 * (j + 1)],
                                                         tmpf32, rstd_c))
                    st["apply_g"][j] = st["apply"]

            def emit_proj_j(cw, j, wbase, rhs_of, dst_of, ev_key, gate_i=False):
                widx = 12 + 16 * cw + wbase + j // 2
                wsl = use_wquarter(widx)
                jj = j % 2
                acc = accs[j % nacc[0]]
                wait_mark(TE, acc_free[j % nacc[0]])
                if ev_key == "h1_ev" and j == 0 and not gate_i:
                    TE.wait_ge(s_ve.h, st["apply"])
                if ev_key == "h2_ev" and j == 0:
                    TE.wait_ge(s_ve.h, st["h1_ev"])
                    if st["h3_mm"]:
                        TE.wait_ge(s_te.h, st["h3_mm"])
                for i in range(NCH):
                    if gate_i:
                        TE.wait_ge(s_ve.h, st["apply_g"][i])
                    ins = TE.matmul(acc[:], wsl[:, i, 128 * jj:128 * (jj + 1)],
                                    rhs_of(i), start=(i == 0), stop=(i == NCH - 1))
                mk = s_te.inc(ins)
                if jj == 1:
                    done_wquarter(widx, mk)
                VE.wait_ge(s_te.h, mk)
                st[ev_key] = s_ve.inc(VE.tensor_copy(dst_of(j), acc[:]))
                acc_free[j % nacc[0]] = (s_ve, st[ev_key])

            def emit_h3_j(cw, j):
                widx = 12 + 16 * cw + 12 + j // 2
                wsl = use_wquarter(widx)
                jj = j % 2
                acc = accs[j % nacc[0]]
                wait_mark(TE, acc_free[j % nacc[0]])
                if j == 0:
                    TE.wait_ge(s_ve.h, st["h2_ev"])
                for i in range(NCH):
                    ins = TE.matmul(acc[:], wsl[:, i, 128 * jj:128 * (jj + 1)],
                                    h2st[:, 512 * i:512 * (i + 1)],
                                    start=(i == 0), stop=(i == NCH - 1))
                mk = s_te.inc(ins)
                st["h3_mm"] = mk
                if jj == 1:
                    done_wquarter(widx, mk)
                VE.wait_ge(s_te.h, mk)
                st["h3_cp"] = s_ve.inc(VE.tensor_copy(xl2[:, 512 * j:512 * (j + 1)], acc[:]))
                acc_free[j % nacc[0]] = (s_ve, st["h3_cp"])

            def emit_finish_j(cw, j):
                SE.wait_ge(s_ve.h, st["h3_cp"])
                st["tanh"] = s_se.inc(SE.activation(
                    xl2[:, 512 * j:512 * (j + 1)], xl2[:, 512 * j:512 * (j + 1)],
                    AF.Tanh, bias=bias_pp[:, j:j + 1], scale=1.0))
                VE.wait_ge(s_se.h, st["tanh"])
                xs = xr_j[:, j, 512 * cw:512 * (cw + 1)]
                st["add"] = s_ve.inc(VE.tensor_add(xs, xs, xl2[:, 512 * j:512 * (j + 1)]))
                SP.wait_ge(s_ve.h, st["add"])
                for h4 in range(4):
                    off = N * j + 512 * cw + 128 * h4
                    src = bass.AP(xr, off, [[NCH * N, 128], [1, 128]])
                    dst = bass.AP(out_d.tensor, off, [[NCH * N, 128], [1, 128]])
                    st_sems[h4].inc(SP.dma_start(dst, src))

            def tail_thunks(cw, flush=False):
                th = []
                if flush:
                    for j in range(NCH):
                        th.append(lambda j=j: emit_stats1_j(cw, j))
                    th.append(lambda: emit_stats1_fin(cw))
                else:
                    for which in range(2):
                        for j in range(NCH):
                            th.append(lambda w=which, j=j: emit_stats_j(cw, w, j))
                        th.append(lambda w=which: emit_stats_fin(cw, w))
                th.append(lambda: emit_rstd(cw))
                for jp in ((0, 1), (2, 3), (4, 5), (6, 7)):
                    th.append(lambda jp=jp: emit_apply(cw, list(jp)))
                for j in range(NCH):
                    th.append(lambda j=j: emit_proj_j(
                        cw, j, 4, lambda i: xl2[:, 512 * i:512 * (i + 1)],
                        lambda j2: qb_ap[:, 2048 * j2 + 512 * cw:
                                         2048 * j2 + 512 * cw + 512], "h1_ev",
                        gate_i=(flush and j == 0)))
                for j in range(NCH):
                    th.append(lambda j=j: emit_proj_j(
                        cw, j, 8,
                        lambda i: qb_ap[:, 2048 * i + 512 * cw:
                                        2048 * i + 512 * cw + 512],
                        lambda j2: h2st[:, 512 * j2:512 * (j2 + 1)], "h2_ev"))
                for j in range(NCH):
                    th.append(lambda j=j: emit_h3_j(cw, j))
                    th.append(lambda j=j: emit_finish_j(cw, j))
                return th

            ao_ready = {}
            work_q = []
            wp_q = []
            work_q_len = [1]
            work_q_done = [0]
            pending_norm = []
            TE.wait_ge(s_ve.h, s_ve.n)
            SE.wait_ge(s_ve.h, cbias_mark)
            emit_qk(0)
            emit_qk(1)
            for t, (c, p, m) in enumerate(iters):
                sl = m % 2
                SE.wait_ge(s_te.h, qk_mark.pop(t))
                if den_slot[sl]:
                    SE.wait_ge(s_te.h, den_slot[sl])
                ins = SE.activation(p_un[sl], scs[sl][:], AF.Exp,
                                    bias=cbias[:], scale=0.125)
                exp_slot[sl] = s_se.inc(ins)
                if t + 2 < len(iters):
                    emit_qk(t + 2)
                    if t + 2 == len(iters) - 2 * NM - 1:
                        preload_flush_kb(qk_mark[t + 2])
                TE.wait_ge(s_se.h, exp_slot[sl])
                if m == 0 and pv_evac_mark:
                    TE.wait_ge(s_ve.h, pv_evac_mark)
                if m == 0 and recip_mark:
                    TE.wait_ge(s_ve.h, recip_mark)
                TE.matmul(pvs[0:64, :],
                          vt[:, 1024 * m + 128 * p: 1024 * m + 128 * p + 64],
                          p_un[sl][:, 0:512],
                          start=(m == 0), stop=(m == NM - 1), tile_position=(0, 0),
                          skip_group_check=True)
                TE.matmul(pvs[64:128, :],
                          vt[:, 1024 * m + 128 * p + 64: 1024 * m + 128 * (p + 1)],
                          p_un[sl][:, 512:1024],
                          start=(m == 0), stop=(m == NM - 1), tile_position=(0, 64),
                          skip_group_check=True)
                TE.matmul(dns[0:64, :], ones64[:], p_un[sl][:, 0:512],
                          start=(m == 0), stop=(m == NM - 1), tile_position=(0, 0),
                          skip_group_check=True)
                ins = TE.matmul(dns[64:128, :], ones64[:], p_un[sl][:, 512:1024],
                                start=(m == 0), stop=(m == NM - 1), tile_position=(0, 64),
                                skip_group_check=True)
                den_slot[sl] = s_te.inc(ins)
                if m == NM - 1:
                    VE.wait_ge(s_te.h, den_slot[sl])
                    if ao_mult_mark:
                        VE.wait_ge(s_ve.h, ao_mult_mark)
                    pv_evac_mark = s_ve.inc(VE.tensor_copy(ao_un, pvs[:]))
                    VE.wait_ge(s_ve.h, pv_evac_mark)
                    dn_cp = s_ve.inc(VE.tensor_copy(dr, dns[:]))
                    recip_mark = dn_cp

                    def _norm(cp=c, pp=p, dncp=dn_cp):
                        SE.wait_ge(s_ve.h, dncp)
                        m1 = s_se.inc(SE.activation(dr, dr, AF.Ln, bias=0.0, scale=1.0))
                        SE.wait_ge(s_se.h, m1)
                        m1 = s_se.inc(SE.activation(dr, dr, AF.Exp, bias=0.0, scale=-1.0))
                        VE.wait_ge(s_se.h, m1)
                        while wp_q:
                            wp_q.pop(0)()
                        if cp > 0:
                            VE.wait_ge(s_te.h, st["wp_mm"])
                        mm = s_ve.inc(VE.tensor_mul(ao[:, 512 * pp:512 * (pp + 1)],
                                                    ao_un, dr))
                        if pp == NCH - 1:
                            ao_ready[cp] = mm
                            wp_q.extend(lambda j=j2: emit_wp_j(cp, j) for j2 in range(NCH))
                            work_q_done[0] = 0
                            work_q.extend(tail_thunks(cp, flush=(cp == NCC - 1)))
                            work_q_len[0] = len(work_q)
                        return mm

                    pending_norm.append(_norm)
                if pending_norm and m == 1:
                    ao_mult_mark = pending_norm.pop(0)()
                if wp_q and m in (2, 4, 6, 8, 10, 12, 14, 0):
                    wp_q.pop(0)()
                it_in_c = NM * p + m
                budget = (it_in_c + 1) * work_q_len[0] // (NM * NCH)
                while work_q and work_q_done[0] < budget:
                    work_q.pop(0)()
                    work_q_done[0] += 1
            preload_flush_vt()
            while pending_norm:
                ao_mult_mark = pending_norm.pop(0)()
            accs.extend([pvs, dns])
            acc_free.extend([(s_ve, ao_mult_mark), (s_ve, ao_mult_mark)])
            nacc[0] = 4
            while wp_q:
                wp_q.pop(0)()
            while work_q:
                work_q.pop(0)()
        for sm in (d_o[0], d_o[1], d_x[0], d_x[1]):
            SP.wait_ge(sm.h, sm.n)
        nc.all_engine_barrier()
    return nc


def _finish_dbg(nc, out_d, xr, dbg_d=None, tensors=None, tiny=None):
    SP = nc.sync
    nc.all_engine_barrier()
    cnt = 0
    with nc.semaphore("dbg") as dbg:
        SP.dma_start(out_d, xr[:]).then_inc(dbg, 16)
        cnt += 16
        if dbg_d:
            qb, kb, vt, S, cbias = tensors
            for nm, t in (("dbg_q", qb), ("dbg_k", kb), ("dbg_s", S), ("dbg_v", vt)):
                SP.dma_start(dbg_d[nm], t[:]).then_inc(dbg, 16)
                cnt += 16
            SP.dma_start(dbg_d["dbg_c"], cbias[:]).then_inc(dbg, 16)
            cnt += 16
        SP.wait_ge(dbg, cnt)
    nc.all_engine_barrier()


_NC_CACHE = {}


def _get_nc():
    if "nc" not in _NC_CACHE:
        _NC_CACHE["nc"] = build_program()
    return _NC_CACHE["nc"]


def kernel(L0_attn_x, L0_attn_Wq, L0_attn_Wk, L0_attn_Wv, L0_attn_Wp,
           L0_ffn_weight, L0_ffn_aux_weight1, L0_ffn_aux_weight2, L0_ffn_bias):
    x = np.ascontiguousarray(np.asarray(L0_attn_x, dtype=np.float32))
    ws = {
        "wq": np.ascontiguousarray(np.asarray(L0_attn_Wq, dtype=np.float32)),
        "wk": np.ascontiguousarray(np.asarray(L0_attn_Wk, dtype=np.float32)),
        "wv": np.ascontiguousarray(np.asarray(L0_attn_Wv, dtype=np.float32)),
        "wp": np.ascontiguousarray(np.asarray(L0_attn_Wp, dtype=np.float32)),
        "w1": np.ascontiguousarray(np.asarray(L0_ffn_weight, dtype=np.float32)),
        "w2": np.ascontiguousarray(np.asarray(L0_ffn_aux_weight1, dtype=np.float32)),
        "w3": np.ascontiguousarray(np.asarray(L0_ffn_aux_weight2, dtype=np.float32)),
    }
    bias = np.ascontiguousarray(np.asarray(L0_ffn_bias, dtype=np.float32))
    identb = np.eye(128, dtype=ml_dtypes.bfloat16)
    identf = np.eye(128, dtype=np.float32)

    nc = _get_nc()
    in_maps = []
    for b in range(8):
        im = {"x": np.ascontiguousarray(x[b].T), "bias": bias,
              "identb": identb, "identf": identf}
        im.update(ws)
        in_maps.append(im)
    res = run_bass_kernel_spmd(nc, in_maps, core_ids=list(range(8)))
    outs = []
    for r in res.results:
        o = r["out"].reshape(128, NCH, N).transpose(1, 0, 2).reshape(D, N)
        outs.append(o.T)
    return np.stack(outs, axis=0)
